# version 17
# baseline (speedup 1.0000x reference)
import numpy as np
import ml_dtypes
import concourse.bass as bass
import concourse.mybir as mybir
from concourse.bass_utils import run_bass_kernel_spmd
from contextlib import ExitStack

F32 = mybir.dt.float32
BF16 = mybir.dt.bfloat16
I32 = mybir.dt.int32
AF = mybir.ActivationFunctionType
ALU = mybir.AluOpType

L = 2048
DM = 2048
NST = 16
TL = 512
NCH = L // TL
NDT = DM // 128
NKT = 16
QN = 4
NQ = NST // QN
EPS_LN = 1e-5
EPS_F = 1e-7
MAXN = 0.99
RD = 1.0 / DM

ENGS = ("sync", "gpsimd", "tensor", "scalar", "vector")
CSEM = {"sync": "cy", "gpsimd": "cg", "tensor": "ct", "scalar": "cs",
        "vector": "cv"}
BARRIER_STREAMS = {"dy_const", "dgc", "dgw"}
SAME_ENGINE_SYNC = True


class Prog:

    def __init__(self):
        self.steps = {e: [] for e in ENGS}
        self.waited = set()
        self.tag_step = {}

    def op(self, eng, fn, waits=(), tag=None, dma=False, stream=None):
        if tag is not None:
            assert tag not in self.tag_step, f"dup tag {tag}"
        assert (stream is not None) == bool(dma), f"dma needs stream: {tag}"
        st = {"eng": eng, "fn": fn, "waits": tuple(waits), "tag": tag,
              "dma": dma, "stream": stream}
        self.steps[eng].append(st)
        if tag is not None:
            self.tag_step[tag] = st
        for w in waits:
            self.waited.add(w)

    def finalize(self):
        for w in self.waited:
            assert w in self.tag_step, f"wait on unknown tag {w}"
        cnt = {}
        self.tagval = {}
        barrier_tags = {}
        for eng in ENGS:
            for st in self.steps[eng]:
                if st["dma"]:
                    st["inc"] = True
                    sem = st["stream"]
                    cnt[sem] = cnt.get(sem, 0) + 16
                    if st["tag"] is not None:
                        self.tagval[st["tag"]] = (sem, cnt[sem])
                        barrier_tags.setdefault(sem, []).append(st["tag"])
                else:
                    need = st["tag"] is not None and st["tag"] in self.waited
                    st["inc"] = need
                    if need:
                        sem = CSEM[eng]
                        cnt[sem] = cnt.get(sem, 0) + 1
                        self.tagval[st["tag"]] = (sem, cnt[sem])
        for s in BARRIER_STREAMS:
            for t in barrier_tags.get(s, ()):
                self.tagval[t] = (s, cnt[s])
        self._check_cycles()
        return sorted(set(CSEM.values()) | set(cnt))

    def _check_cycles(self):
        n = 0
        nid = {}
        for eng in ENGS:
            for st in self.steps[eng]:
                nid[id(st)] = n
                n += 1
                if st["dma"]:
                    n += 1
        adj = [[] for _ in range(n)]
        indeg = [0] * n

        def edge(a, b):
            adj[a].append(b)
            indeg[b] += 1

        stream_steps = {}
        for eng in ENGS:
            for st in self.steps[eng]:
                if st["dma"]:
                    stream_steps.setdefault(st["stream"], []).append(st)
        for eng in ENGS:
            prev = None
            for st in self.steps[eng]:
                a = nid[id(st)]
                if prev is not None:
                    edge(prev, a)
                prev = a
                if st["dma"]:
                    edge(a, a + 1)
                for w in st["waits"]:
                    src = self.tag_step[w]
                    if src["dma"] and src["stream"] in BARRIER_STREAMS:
                        for s2 in stream_steps[src["stream"]]:
                            edge(nid[id(s2)] + 1, a)
                    else:
                        s = nid[id(src)]
                        if src["dma"]:
                            s += 1
                        edge(s, a)
        from collections import deque
        q = deque(i for i in range(n) if indeg[i] == 0)
        seen = 0
        while q:
            u = q.popleft()
            seen += 1
            for v in adj[u]:
                indeg[v] -= 1
                if indeg[v] == 0:
                    q.append(v)
        assert seen == n, f"DEADLOCK: dependency cycle ({n - seen} stuck)"

    def emit_engine(self, eng_handle, eng_name, sems):
        hw = {}
        myc = CSEM[eng_name]
        for st in self.steps[eng_name]:
            for w in st["waits"]:
                sem, val = self.tagval[w]
                if not SAME_ENGINE_SYNC and sem == myc:
                    continue
                if hw.get(sem, -1) >= val:
                    continue
                eng_handle.wait_ge(sems[sem], val)
                hw[sem] = val
            ins = st["fn"](eng_handle)
            if st["inc"]:
                sem = st["stream"] if st["dma"] else myc
                ins.then_inc(sems[sem], 16 if st["dma"] else 1)


class K:

    def __init__(self):
        self.P = Prog()

    def build(self):
        nc = bass.Bass("TRN2", target_bir_lowering=False)
        self.nc = nc
        s = self

        s.xT_d = nc.dram_tensor("xT", [DM, L], F32, kind="ExternalInput").ap()
        s.xpw_d = nc.dram_tensor("xpw", [128, NDT, 160], F32,
                                 kind="ExternalInput").ap()
        s.dtw_d = nc.dram_tensor("dtw", [128, DM], F32,
                                 kind="ExternalInput").ap()
        s.dtb_d = nc.dram_tensor("dtb", [128, NDT], F32,
                                 kind="ExternalInput").ap()
        s.alog_d = nc.dram_tensor("alog", [128, NDT * NST], F32,
                                  kind="ExternalInput").ap()
        s.dp_d = nc.dram_tensor("dp", [128, NDT], F32,
                                kind="ExternalInput").ap()
        s.wsw_d = nc.dram_tensor("wsw", [NKT, 128, NDT, 128], F32,
                                 kind="ExternalInput").ap()
        s.scl_d = nc.dram_tensor("scl", [128, NKT], F32,
                                 kind="ExternalInput").ap()
        s.eye_d = nc.dram_tensor("eye", [128, 128], BF16,
                                 kind="ExternalInput").ap()
        s.onesc_d = nc.dram_tensor("onesc", [128, 1], BF16,
                                   kind="ExternalInput").ap()
        s.out_d = nc.dram_tensor("out", [DM, L], F32,
                                 kind="ExternalOutput").ap()
        s.wbf_d = nc.dram_tensor("wbf", [NKT, 128, NDT, 128], BF16,
                                 kind="Internal").ap()
        s.bcscr_d = nc.dram_tensor("bcscr", [32, TL], BF16, kind="Internal").ap()
        s.bcs2 = s.bcscr_d.rearrange("(g n) t -> g (n t)", g=2)
        s.rbscr_d = nc.dram_tensor("rbscr", [1, 4 * TL], BF16,
                                   kind="Internal").ap()
        s.rb4 = s.rbscr_d.rearrange("o (r t) -> (o r) t", r=4)
        s.rb2 = s.rbscr_d.rearrange("o (r t) -> (o r) t", r=2)

        with ExitStack() as ctx:
            def sb(name, shape, dt):
                return ctx.enter_context(nc.sbuf_tensor(name, shape, dt))

            s.c_xpw = sb("c_xpw", [128, NDT, 160], BF16)
            s.c_dtw = sb("c_dtw", [128, DM], BF16)
            s.c_dtb = sb("c_dtb", [128, NDT], F32)
            s.c_A = sb("c_A", [128, NDT * NST], F32)
            s.c_dp = sb("c_dp", [128, NDT], F32)
            s.c_D1 = sb("c_D1", [128, NDT], F32)
            s.c_diag = sb("c_diag", [128, NDT, 128], BF16)
            s.c_scl = sb("c_scl", [128, NKT], F32)
            s.eye_b = sb("eye_b", [128, 128], BF16)
            s.onesc_b = sb("onesc_b", [128, 1], BF16)
            s.idx0 = sb("idx0", [128, 1], I32)
            s.idx1 = sb("idx1", [128, 1], I32)
            s.c_eps = sb("c_eps", [128, 2], F32)
            s.state_sb = sb("state_sb", [128, NDT, NST], BF16)

            s.x_b = sb("x_b", [128, NDT, TL], BF16)
            s.delta_b = sb("delta_b", [128, NDT, TL], BF16)
            s.du_b = sb("du_b", [128, TL], BF16)
            s.bc32 = sb("bc32", [32, TL], BF16)
            s.t1_2 = sb("t1_2", [128, TL], BF16)
            s.u_2 = sb("u_2", [128, TL], BF16)
            s.B_rep = sb("B_rep", [128, NST, TL], BF16)
            s.C_rep = sb("C_rep", [128, NST, TL], BF16)
            s.dae_x = sb("dae_x", [128, 3, QN, TL + 1], F32)
            s.bbh = sb("bbh", [128, 2, QN, TL + 1], BF16)
            s.g_x = sb("g_x", [128, 4, QN, TL], BF16)
            s.r_b = sb("r_b", [128, NDT, TL], BF16)
            s.rsq_2 = sb("rsq_2", [128, TL], BF16)
            s.rc_b = sb("rc_b", [128, NDT, TL], BF16)
            s.w_sl = sb("w_sl", [128, 3, NDT // 2, 128], BF16)
            s.e_b = sb("e_b", [128, NKT, TL], BF16)
            s.esq_2 = sb("esq_2", [128, TL], BF16)
            s.tf_2 = sb("tf_2", [128, TL], BF16)
            s.out_2 = sb("out_2", [128, TL], F32)
            s.rows = sb("rows", [1, 7, TL], F32)
            s.rep_sb = sb("rep_sb", [128, 3, TL], BF16)

            pb1 = ctx.enter_context(nc.psum_tensor("pb1", [128, 2048], F32))
            pb2 = ctx.enter_context(nc.psum_tensor("pb2", [128, 2048], F32))
            s.ps_du = pb1[:, 0:512]
            s.ps_bc = pb1[0:32, 512:1024]
            s.ps_dpre = lambda p: pb1[:, p * 512:(p + 1) * 512]
            s.ps_f = lambda p: pb1[:, 1024 + p * 512:1536 + p * 512]
            s.ps_r = lambda p: pb2[:, p * 512:(p + 1) * 512]
            s.ps_s6 = pb2[0:1, 1024:1536]
            s.ps_s7 = pb2[0:1, 1536:2048]

            s.row = lambda i: s.rows[0:1, i, :]

            s.emit_all()

            sems_needed = s.P.finalize()
            with ExitStack() as sctx:
                sems = {nm: sctx.enter_context(nc.semaphore(nm))
                        for nm in sems_needed}
                block = sctx.enter_context(nc.Block())
                for name in ENGS:
                    getattr(block, name)(
                        lambda eng, name=name: s.P.emit_engine(eng, name, sems))
        return nc

    def emit_all(s):
        s.emit_setup()
        s.emit_p1(0)
        for dt in range(NDT):
            s.emit_p2_core(0, dt)
            if dt >= 1:
                s.emit_p2_tail(0, dt - 1)
            if dt >= 2:
                s.emit_rev(0, dt - 2)
        s.emit_p2_tail(0, NDT - 1)
        s.emit_rev(0, NDT - 2)
        s.emit_rev(0, NDT - 1)
        s.emit_stats(0)
        s.emit_p3_rows(0)
        s.emit_p3_rc(0)
        s.emit_p1(1)
        for c in range(NCH):
            nxt = c + 1 if c + 1 < NCH else None

            def wv(i, base):
                dt = base + i
                s.emit_p2_core(nxt, dt)
                if dt >= 1:
                    s.emit_p2_tail(nxt, dt - 1)
                if dt >= 2:
                    s.emit_rev(nxt, dt - 2)
            for i in range(8):
                if nxt is not None:
                    wv(i, 0)
                s.emit_p4_kt(c, 2 * i)
                s.emit_p4_kt(c, 2 * i + 1)
            if nxt is not None:
                wv(0, 8)
                wv(1, 8)
            s.emit_p5a(c)
            for i in range(8):
                if nxt is not None and i + 2 < 8:
                    wv(i + 2, 8)
                s.emit_p5_kt(c, 2 * i)
                s.emit_p5_kt(c, 2 * i + 1)
            if nxt is not None:
                s.emit_p2_tail(nxt, NDT - 1)
                s.emit_rev(nxt, NDT - 2)
                s.emit_rev(nxt, NDT - 1)
            if c + 2 < NCH:
                s.emit_p1_dma(c + 2)
            s.emit_p5b(c)
            if nxt is not None:
                s.emit_stats(nxt)
                s.emit_p3_rows(nxt)
            if c + 2 < NCH:
                s.emit_p1_comp(c + 2)
            for kt in range(NKT // 2):
                s.emit_out_kt(c, kt)
            if nxt is not None:
                s.emit_p3_rc(nxt)
            for kt in range(NKT // 2, NKT):
                s.emit_out_kt(c, kt)
        last = [("st_out", NCH - 1, NKT - 2), ("st_out", NCH - 1, NKT - 1)]
        s.P.op("vector", lambda e: e.engine_nop(), waits=last, tag="FINAL")

    def emit_setup(s):
        P = s.P
        P.op("sync", lambda e: e.dma_start(out=s.c_dtb[:], in_=s.dtb_d),
             dma=True, tag="L_dtb", stream="dy_const")
        P.op("sync", lambda e: e.dma_start(out=s.c_A[:], in_=s.alog_d),
             dma=True, tag="L_alog", stream="dy_const")
        P.op("sync", lambda e: e.dma_start(out=s.c_dp[:], in_=s.dp_d),
             dma=True, tag="L_dp", stream="dy_const")
        P.op("sync", lambda e: e.dma_start(out=s.c_scl[:], in_=s.scl_d),
             dma=True, tag="L_scl", stream="dy_const")
        P.op("sync", lambda e: e.dma_start(out=s.eye_b[:], in_=s.eye_d),
             dma=True, tag="L_eye", stream="dy_const")
        P.op("sync", lambda e: e.dma_start(out=s.onesc_b[:], in_=s.onesc_d),
             dma=True, tag="L_onesc", stream="dy_const")
        P.op("gpsimd", lambda e: e.dma_start(out=s.c_xpw[:], in_=s.xpw_d),
             dma=True, tag="L_xpw", stream="dgc")
        P.op("gpsimd", lambda e: e.dma_start(out=s.c_dtw[:], in_=s.dtw_d),
             dma=True, tag="L_dtw", stream="dgc")
        for s4 in range(4):
            P.op("gpsimd", lambda e, s4=s4: e.dma_start(
                out=s.wbf_d[s4 * 4:(s4 + 1) * 4],
                in_=s.wsw_d[s4 * 4:(s4 + 1) * 4]),
                dma=True, tag=("wcast", s4), stream="dgw")
        P.op("gpsimd", lambda e: e.memset(s.state_sb[:], 0.0), tag="M_state")
        P.op("gpsimd", lambda e: e.memset(s.dae_x[:, 0, :, 0:1], 0.0),
             tag="M_dae0")
        P.op("gpsimd", lambda e: e.memset(s.dae_x[:, 1, :, 0:1], 0.0),
             tag="M_dae1")
        P.op("gpsimd", lambda e: e.memset(s.dae_x[:, 2, :, 0:1], 0.0),
             tag="M_dae2")
        P.op("gpsimd", lambda e: e.memset(s.idx0[:], 0), tag="M_idx0")
        P.op("gpsimd", lambda e: e.memset(s.idx1[:], 1), tag="M_idx1")
        P.op("gpsimd", lambda e: e.memset(s.c_eps[:, 0:1], EPS_LN),
             tag="M_eps")
        P.op("gpsimd", lambda e: e.memset(
            s.rows[0:1, 0:4, :].rearrange("o k t -> o (k t)"), 0.0),
            tag="M_rb0")
        P.op("gpsimd", lambda e: e.dma_start(
            out=s.rbscr_d,
            in_=s.rows[0:1, 0:4, :].rearrange("o k t -> o (k t)")),
            waits=["M_rb0"], tag="M_rbz", dma=True, stream="dgz")
        P.op("scalar", lambda e: e.activation(s.c_A[:], s.c_A[:], AF.Exp),
             waits=["L_alog"], tag="S_Aexp")
        P.op("vector", lambda e: e.tensor_scalar(
            s.c_A[:], s.c_A[:], -1.0, None, ALU.mult), waits=["S_Aexp"],
            tag="S_Aneg")
        P.op("scalar", lambda e: e.add(s.c_D1[:], s.c_dp[:], 1.0),
             waits=["L_dp"], tag="S_D1")
        for dt in range(NDT):
            P.op("vector", lambda e, dt=dt: e.tensor_scalar(
                s.c_diag[:, dt, :], s.eye_b[:], s.c_D1[:, dt:dt + 1], None,
                ALU.mult), waits=["L_eye", "S_D1"], tag=("S_diag", dt))

    def emit_p1(s, c):
        s.emit_p1_dma(c)
        s.emit_p1_comp(c)

    def emit_p1_dma(s, c):
        P = s.P
        xin = s.xT_d.rearrange("(dt p) l -> p dt l", p=128)[
            :, :, c * TL:(c + 1) * TL]
        xw = []
        if c > 0:
            xw = [("diagmm", c - 1, NDT - 1), ("u", c - 1, NDT - 1)]
        P.op("gpsimd", lambda e: e.dma_start(out=s.x_b[:], in_=xin),
             waits=xw, tag=("xload", c), dma=True, stream="dgx")

    def emit_p1_comp(s, c):
        P = s.P
        for dt in range(NDT):
            w = [("xload", c), "L_xpw"]
            if c > 0 and dt == 0:
                w += [("sp1", c - 1, NDT - 2)]
            P.op("tensor", lambda e, dt=dt: e.matmul(
                s.ps_du, s.c_xpw[:, dt, 0:128], s.x_b[:, dt, :],
                start=(dt == 0), stop=(dt == NDT - 1)), waits=w,
                tag=("xdbl", c, dt))
            w2 = []
            if c > 0 and dt == 0:
                w2 = [("sp1", c - 1, NDT - 1)]
            P.op("tensor", lambda e, dt=dt: e.matmul(
                s.ps_bc, s.c_xpw[:, dt, 128:160], s.x_b[:, dt, :],
                start=(dt == 0), stop=(dt == NDT - 1)), waits=w2,
                tag=("xdblbc", c, dt))
        w = [("xdbl", c, NDT - 1)]
        if c > 0:
            w.append(("dpre", c - 1, NDT - 1))
        P.op("scalar", lambda e: e.activation(s.du_b[:], s.ps_du, AF.Copy),
             waits=w, tag=("du_ev", c))
        P.op("scalar", lambda e: e.activation(s.bc32[:], s.ps_bc, AF.Copy),
             waits=[("xdblbc", c, NDT - 1)], tag=("bc_ev", c))
        scw = [("bc_ev", c)]
        if c > 0:
            scw += [("gatB", c - 1), ("gatC", c - 1)]
        P.op("gpsimd", lambda e: e.dma_start(out=s.bcscr_d, in_=s.bc32[:]),
             waits=scw, tag=("bscr", c), dma=True, stream="dgs")
        gw = [("bscr", c), "M_idx0"]
        if c > 0:
            gw += [("bb", c - 1, NDT - 1, NQ - 1),
                   ("bb", c - 1, NDT - 1, 0)]
        P.op("gpsimd", lambda e: e.indirect_dma_start(
            out=s.B_rep[:].rearrange("p n t -> p (n t)"), out_offset=None,
            in_=s.bcs2,
            in_offset=bass.IndirectOffsetOnAxis(s.idx0[:, 0:1], axis=0)),
            waits=gw, tag=("gatB", c), dma=True, stream="dgb")
        gw2 = [("bscr", c), "M_idx1"]
        if c > 0:
            gw2 += [("cmul", c - 1, NDT - 1, NQ - 1),
                    ("cmul", c - 1, NDT - 1, 0)]
        P.op("gpsimd", lambda e: e.indirect_dma_start(
            out=s.C_rep[:].rearrange("p n t -> p (n t)"), out_offset=None,
            in_=s.bcs2,
            in_offset=bass.IndirectOffsetOnAxis(s.idx1[:, 0:1], axis=0)),
            waits=gw2, tag=("gatC", c), dma=True, stream="dgg")
        for dt in range(NDT):
            par = dt % 2
            w = [("du_ev", c), "L_dtw"]
            if dt == 1:
                w.append(("bc_ev", c))
            if dt >= 2:
                w.append(("sp1", c, dt - 2))
            P.op("tensor", lambda e, dt=dt, par=par: e.matmul(
                s.ps_dpre(par), s.c_dtw[:, dt * 128:(dt + 1) * 128],
                s.du_b[:], start=True, stop=True), waits=w,
                tag=("dpre", c, dt))
            w = [("dpre", c, dt), "L_dtb"]
            if dt >= 1:
                w.append(("sp2", c, dt - 1))
            P.op("scalar", lambda e, dt=dt, par=par: e.activation(
                s.t1_2[:], s.ps_dpre(par), AF.Exp,
                bias=s.c_dtb[:, dt:dt + 1]), waits=w, tag=("sp1", c, dt))
            w = [("sp1", c, dt)]
            if c > 0:
                w.append(("u", c - 1, dt))
            P.op("scalar", lambda e, dt=dt, par=par: e.activation(
                s.delta_b[:, dt, :], s.t1_2[:], AF.Ln, bias=1.0),
                waits=w, tag=("sp2", c, dt))

    def emit_p2_core(s, c, dt):
        P = s.P
        w = [("sp2", c, dt), ("xload", c)]
        if dt >= 1:
            w += [("bb", c, dt - 1, NQ - 1), ("bb", c, dt - 1, 0)]
        elif c > 0:
            w += [("bb", c - 1, NDT - 1, NQ - 1),
                  ("bb", c - 1, NDT - 1, 0)]
        P.op("vector", lambda e, dt=dt: e.tensor_tensor(
            s.u_2[:], s.delta_b[:, dt, :], s.x_b[:, dt, :],
            ALU.mult), waits=w, tag=("u", c, dt))

        for q in range(NQ):
            gidx = (c * NDT + dt) * NQ + q
            par = gidx % 2
            pg = gidx - 2
            pc, pdt, pq = (pg // (NDT * NQ), (pg // NQ) % NDT, pg % NQ)
            prev = pg >= 0
            bbeng = "gpsimd" if q == 0 else "vector"

            dpar = gidx % 3
            g4d = gidx - 3
            for j in range(QN):
                n = q * QN + j
                w = [("sp2", c, dt), "S_Aneg", "M_dae0", "M_dae1", "M_dae2"]
                if g4d >= 0 and j == 0:
                    w.append(("scan", g4d // (NDT * NQ),
                              (g4d // NQ) % NDT, g4d % NQ))
                P.op("scalar", lambda e, dt=dt, dpar=dpar, j=j, n=n:
                     e.activation(s.dae_x[:, dpar, j, 1:TL + 1],
                                  s.delta_b[:, dt, :], AF.Exp,
                                  scale=s.c_A[:, dt * NST + n:
                                              dt * NST + n + 1]),
                     waits=w, tag=("dae", c, dt, q, j))

            w = [("u", c, dt), ("gatB", c)]
            if prev:
                w += [("cmul", pc, pdt, pq), ("cout", pc, pdt, pq)]
            P.op(bbeng, lambda e, dt=dt, q=q, par=par: e.tensor_tensor(
                s.bbh[:, par, :, 1:TL + 1],
                s.u_2[:].unsqueeze(1).broadcast_to([128, QN, TL]),
                s.B_rep[:, q * QN:(q + 1) * QN, :], ALU.mult),
                waits=w, tag=("bb", c, dt, q))

            w = ["M_state"] if c == 0 else [("cout", c - 1, dt, q)]
            if prev:
                w.append(("scan", pc, pdt, pq))
            P.op("vector", lambda e, dt=dt, q=q, par=par: e.tensor_copy(
                s.bbh[:, par, :, 0:1],
                s.state_sb[:, dt, q * QN:(q + 1) * QN].unsqueeze(2)),
                waits=w, tag=("cin", c, dt, q))

            w = [("dae", c, dt, q, QN - 1), ("bb", c, dt, q),
                 ("cin", c, dt, q)]
            P.op("vector", lambda e, par=par, dpar=gidx % 3:
                 e.tensor_tensor_scan(
                s.bbh[:, par].rearrange("p q t -> p (q t)"),
                s.dae_x[:, dpar].rearrange("p q t -> p (q t)"),
                s.bbh[:, par].rearrange("p q t -> p (q t)"),
                0.0, ALU.mult, ALU.add),
                waits=w, tag=("scan", c, dt, q))

            P.op("vector", lambda e, dt=dt, q=q, par=par: e.tensor_copy(
                s.state_sb[:, dt, q * QN:(q + 1) * QN].unsqueeze(2),
                s.bbh[:, par, :, TL:TL + 1]),
                waits=[("scan", c, dt, q)], tag=("cout", c, dt, q))

            w = [("scan", c, dt, q), ("gatC", c)]
            g4 = gidx - 4
            if g4 >= 0:
                w.append(("idmm", g4 // (NDT * NQ), (g4 // NQ) % NDT,
                          g4 % NQ, QN - 1))
            gpar = gidx % 4
            P.op(bbeng, lambda e, dt=dt, q=q, par=par, gpar=gpar:
                 e.tensor_tensor(
                s.g_x[:, gpar, :, :], s.bbh[:, par, :, 1:TL + 1],
                s.C_rep[:, q * QN:(q + 1) * QN, :], ALU.mult),
                waits=w, tag=("cmul", c, dt, q))

    def emit_p2_tail(s, c, dt):
        P = s.P
        for q in range(NQ):
            gpar = ((c * NDT + dt) * NQ + q) % 4
            for j in range(QN):
                w = [("cmul", c, dt, q), "L_eye"]
                if q == 0 and j == 0 and c * NDT + dt >= 2:
                    rdt, rc_ = dt - 2, c
                    if rdt < 0:
                        rdt += NDT
                        rc_ -= 1
                    w.append(("rev", rc_, rdt))
                P.op("tensor", lambda e, q=q, j=j, gpar=gpar, dt=dt: e.matmul(
                    s.ps_r(dt % 2), s.eye_b[:], s.g_x[:, gpar, j, :],
                    start=(q == 0 and j == 0), stop=False),
                    waits=w, tag=("idmm", c, dt, q, j))

        P.op("tensor", lambda e, dt=dt: e.matmul(
            s.ps_r(dt % 2), s.c_diag[:, dt, :], s.x_b[:, dt, :],
            start=False, stop=True),
            waits=[("xload", c), ("S_diag", dt)], tag=("diagmm", c, dt))

    def emit_rev(s, c, dt):
        P = s.P
        w = [("diagmm", c, dt)]
        if c > 0:
            w.append(("rc", c - 1, dt))
        P.op("scalar", lambda e, dt=dt: e.activation(
            s.r_b[:, dt, :], s.ps_r(dt % 2), AF.Copy), waits=w,
            tag=("rev", c, dt))

    def emit_stats(s, c):
        P = s.P
        for dt in range(NDT):
            w = [("rev", c, dt), "L_onesc"]
            if dt == 0 and c > 0:
                w.append(("s14ev", c - 1))
            P.op("tensor", lambda e, dt=dt: e.matmul(
                s.ps_s6, s.onesc_b[:], s.r_b[:, dt, :],
                start=(dt == 0), stop=(dt == NDT - 1)), waits=w,
                tag=("smr", c, dt))
            w = [("rev", c, dt)]
            if dt >= 1:
                w.append(("smr2", c, dt - 1))
            elif c > 0:
                w.append(("smr2", c - 1, NDT - 1))
            P.op("scalar", lambda e, dt=dt: e.activation(
                s.rsq_2[:], s.r_b[:, dt, :], AF.Square),
                waits=w, tag=("rsq", c, dt))
            w = [("rsq", c, dt)]
            if dt == 0 and c > 0:
                w.append(("s15ev", c - 1))
            P.op("tensor", lambda e, dt=dt: e.matmul(
                s.ps_s7, s.onesc_b[:], s.rsq_2[:],
                start=(dt == 0), stop=(dt == NDT - 1)), waits=w,
                tag=("smr2", c, dt))

    def emit_p3_rows(s, c):
        P = s.P
        row = s.row
        w = [("smr", c, NDT - 1)]
        w.append(("mean2", c - 1) if c > 0 else "M_rbz")
        P.op("vector", lambda e: e.tensor_copy(row(0), s.ps_s6),
             waits=w, tag=("rows1", c))
        w = [("smr2", c, NDT - 1)]
        if c > 0:
            w += [("mrs2", c - 1), ("wln2", c - 1)]
        P.op("vector", lambda e: e.tensor_copy(row(1), s.ps_s7),
             waits=w, tag=("rows2", c))
        w = [("rows1", c)]
        if c > 0:
            w.append(("wln2", c - 1))
        P.op("vector", lambda e: e.tensor_scalar(
            row(2), row(0), RD, None, ALU.mult), waits=w,
            tag=("mean", c))
        w = [("rows2", c)]
        if c > 0:
            w.append(("rstd2", c - 1))
        P.op("vector", lambda e: e.tensor_scalar(
            row(3), row(1), RD, None, ALU.mult), waits=w,
            tag=("er2", c))
        w = [("mean", c)]
        if c > 0:
            w.append(("lnv2", c - 1))
        P.op("vector", lambda e: e.tensor_tensor(
            row(4), row(2), row(2), ALU.mult), waits=w,
            tag=("m2", c))
        P.op("vector", lambda e: e.tensor_tensor(
            row(4), row(3), row(4), ALU.subtract),
            waits=[("m2", c), ("er2", c)], tag=("var", c))
        P.op("scalar", lambda e: e.activation(
            row(3), row(4), AF.Ln, bias=s.c_eps[0:1, 0:1]),
            waits=[("var", c), "M_eps"], tag=("lnv", c))
        w = [("lnv", c)]
        if c > 0:
            w.append(("wln2", c - 1))
        P.op("scalar", lambda e: e.activation(
            row(5), row(3), AF.Exp, scale=-0.5),
            waits=w, tag=("rstd", c))
        w = [("mean", c)]
        w.append(("gmean", c - 1) if c > 0 else "M_rbz")
        P.op("gpsimd", lambda e: e.dma_start(
            out=s.rbscr_d[0:1, 0:TL], in_=row(2)), waits=w,
            tag=("wmean", c), dma=True, stream="dr1")
        w = [("wmean", c), "M_idx0"]
        if c > 0:
            w.append(("rc", c - 1, NDT - 1))
        P.op("gpsimd", lambda e: e.indirect_dma_start(
            out=s.rep_sb[:, 2, :], out_offset=None,
            in_=s.rb4,
            in_offset=bass.IndirectOffsetOnAxis(s.idx0[:, 0:1], axis=0)),
            waits=w, tag=("gmean", c), dma=True, stream="dh1")

    def emit_p3_rc(s, c):
        P = s.P
        for dt in range(NDT):
            w = [("gmean", c), ("rev", c, dt)]
            if c > 0:
                w.append(("ffnmm", c - 1, NKT - 1, dt))
            P.op("vector", lambda e, dt=dt: e.tensor_tensor(
                s.rc_b[:, dt, :], s.r_b[:, dt, :], s.rep_sb[:, 2, :],
                ALU.subtract), waits=w, tag=("rc", c, dt))

    def emit_p4_kt(s, c, kt):
        P = s.P
        gk = c * NKT + kt
        pk, pc2 = kt - 2, c
        if pk < 0:
            pk += NKT
            pc2 -= 1
        for h in range(2):
            hs = (gk * 2 + h) - 3
            w = [("wcast", kt // 4)]
            if hs >= 0:
                okt, oh = (hs // 2) % NKT, hs % 2
                oc = hs // (2 * NKT)
                w.append(("ffnmm", oc, okt, oh * 8 + 7))
            P.op("sync", lambda e, kt=kt, h=h, gk=gk: e.dma_start(
                out=s.w_sl[:, (gk * 2 + h) % 3, :, :],
                in_=s.wbf_d[kt, :, h * 8:(h + 1) * 8, :]), waits=w,
                tag=("wld", c, kt, h), dma=True,
                stream="dw%d" % ((gk * 2 + h) % 3))
        for dt in range(NDT):
            w = [("wld", c, kt, dt // 8), ("rc", c, dt)]
            if dt == 0:
                if gk >= 2:
                    w.append(("scl_ev", pc2, pk))
                if kt == 0:
                    w += [("sp1", c, NDT - 1), ("sp1", c, NDT - 2)]
            P.op("tensor", lambda e, kt=kt, dt=dt, gk=gk: e.matmul(
                s.ps_f(kt % 2),
                s.w_sl[:, (gk * 2 + dt // 8) % 3, dt % 8, :],
                s.rc_b[:, dt, :],
                start=(dt == 0), stop=(dt == NDT - 1)), waits=w,
                tag=("ffnmm", c, kt, dt))
        w = [("ffnmm", c, kt, NDT - 1), "L_scl"]
        if c > 0:
            w += [("o1", c - 1, kt), ("smq", c - 1, kt), ("q2", c - 1, kt)]
        P.op("scalar", lambda e, kt=kt: e.activation(
            s.e_b[:, kt, :], s.ps_f(kt % 2), AF.Copy,
            scale=s.c_scl[:, kt:kt + 1]), waits=w, tag=("scl_ev", c, kt))
        w = [("scl_ev", c, kt)]
        if kt >= 1:
            w.append(("sme", c, kt - 1))
        elif c > 0:
            w.append(("smq2", c - 1, NKT - 1))
        P.op("scalar", lambda e, kt=kt: e.activation(
            s.esq_2[:], s.e_b[:, kt, :], AF.Square), waits=w,
            tag=("esq", c, kt))
        w = [("esq", c, kt)]
        if kt == 0:
            w.append(("rows1", c))
        P.op("tensor", lambda e, kt=kt: e.matmul(
            s.ps_s6, s.onesc_b[:], s.esq_2[:],
            start=(kt == 0), stop=(kt == NKT - 1)), waits=w,
            tag=("sme", c, kt))

    def emit_p5a(s, c):
        P = s.P
        row = s.row
        P.op("vector", lambda e: e.tensor_copy(row(0), s.ps_s6),
             waits=[("sme", c, NKT - 1), ("m2", c)], tag=("s9ev", c))
        P.op("scalar", lambda e: e.activation(row(1), row(0), AF.Ln),
             waits=[("s9ev", c), ("er2", c)], tag=("lnn", c))
        w = [("lnn", c), ("wmean", c)]
        if c > 0:
            w.append(("wln2", c - 1))
        w.append(("m2", c))
        P.op("scalar", lambda e: e.activation(row(2), row(1), AF.Exp,
                                              scale=0.5),
             waits=w, tag=("qn", c))
        P.op("vector", lambda e: e.tensor_tensor(
            row(3), row(5), row(2), ALU.mult),
            waits=[("qn", c), ("rstd", c)], tag=("vnr", c))
        P.op("vector", lambda e: e.tensor_scalar(
            row(4), row(3), EPS_F, None, ALU.max),
            waits=[("vnr", c), ("lnv", c)], tag=("vn", c))
        P.op("vector", lambda e: e.tensor_scalar(
            row(1), row(4), 15.0, None, ALU.min), waits=[("vn", c)],
            tag=("vnc", c))
        P.op("scalar", lambda e: e.activation(row(1), row(1), AF.Exp,
                                              scale=2.0),
             waits=[("vnc", c)], tag=("te", c))
        P.op("vector", lambda e: e.tensor_scalar(
            row(1), row(1), 1.0, None, ALU.add), waits=[("te", c)],
            tag=("te1", c))
        P.op("scalar", lambda e: e.activation(row(0), row(1), AF.Ln),
             waits=[("te1", c)], tag=("te2a", c))
        P.op("scalar", lambda e: e.activation(row(0), row(0), AF.Exp,
                                              scale=-1.0),
             waits=[("te2a", c)], tag=("te2", c))
        P.op("vector", lambda e: e.tensor_scalar(
            row(1), row(0), -2.0, 1.0, ALU.mult, ALU.add),
            waits=[("te2", c)], tag=("tanh", c))
        P.op("vector", lambda e: e.tensor_scalar(
            row(0), row(4), EPS_F, None, ALU.add),
            waits=[("vn", c), ("tanh", c)], tag=("dn", c))
        P.op("scalar", lambda e: e.activation(row(2), row(0), AF.Ln),
             waits=[("dn", c)], tag=("dnra", c))
        P.op("scalar", lambda e: e.activation(row(2), row(2), AF.Exp,
                                              scale=-1.0),
             waits=[("dnra", c)], tag=("dnr", c))
        P.op("vector", lambda e: e.tensor_tensor(
            row(1), row(1), row(2), ALU.mult),
            waits=[("tanh", c), ("dnr", c)], tag=("f1", c))
        P.op("vector", lambda e: e.tensor_tensor(
            row(0), row(1), row(3), ALU.mult),
            waits=[("f1", c), ("dnra", c)], tag=("onorm0", c))
        P.op("vector", lambda e: e.tensor_scalar(
            row(0), row(0), EPS_F, None, ALU.max), waits=[("onorm0", c)],
            tag=("onorm", c))
        P.op("scalar", lambda e: e.activation(row(2), row(0), AF.Ln),
             waits=[("onorm", c), ("f1", c)], tag=("onra", c))
        P.op("scalar", lambda e: e.activation(row(2), row(2), AF.Exp,
                                              scale=-1.0),
             waits=[("onra", c)], tag=("onr", c))
        P.op("vector", lambda e: e.tensor_scalar(
            row(2), row(2), MAXN, 1.0, ALU.mult, ALU.min),
            waits=[("onr", c)], tag=("f2", c))
        P.op("vector", lambda e: e.tensor_tensor(
            row(0), row(1), row(2), ALU.mult),
            waits=[("f2", c), ("onra", c)], tag=("ff", c))
        w = [("ff", c)]
        w.append(("gfac", c - 1) if c > 0 else "M_rbz")
        P.op("gpsimd", lambda e: e.dma_start(
            out=s.rbscr_d[0:1, TL:2 * TL], in_=row(0)), waits=w,
            tag=("wfac", c), dma=True, stream="dr2")
        w = [("wfac", c), "M_idx1"]
        if c > 0:
            w += [("tmix", c - 1, NKT - 1), ("outtt", c - 1, NKT - 1)]
        P.op("gpsimd", lambda e: e.indirect_dma_start(
            out=s.rep_sb[:, 1, :], out_offset=None,
            in_=s.rb4,
            in_offset=bass.IndirectOffsetOnAxis(s.idx1[:, 0:1], axis=0)),
            waits=w, tag=("gfac", c), dma=True, stream="dh2")

    def emit_p5_kt(s, c, kt):
        P = s.P
        w = [("gfac", c), ("scl_ev", c, kt)]
        if kt >= 1:
            w.append(("r2", c, kt - 1))
        elif c > 0:
            w.append(("outtt", c - 1, NKT - 1))
        P.op("vector", lambda e, kt=kt: e.tensor_tensor(
            s.tf_2[:], s.e_b[:, kt, :], s.rep_sb[:, 1, :],
            ALU.mult), waits=w, tag=("tmix", c, kt))
        P.op("vector", lambda e, kt=kt: e.tensor_tensor(
            s.e_b[:, kt, :], s.tf_2[:], s.rc_b[:, kt, :],
            ALU.add),
            waits=[("tmix", c, kt), ("esq", c, kt), ("rc", c, kt)],
            tag=("r2", c, kt))
        w = [("r2", c, kt), ("sme", c, NKT - 1)]
        if kt >= 1:
            w.append(("smq2", c, kt - 1))
        P.op("scalar", lambda e, kt=kt: e.activation(
            s.esq_2[:], s.e_b[:, kt, :], AF.Square), waits=w,
            tag=("q2", c, kt))
        w = [("r2", c, kt)]
        if kt == 0:
            w.append(("s9ev", c))
        P.op("tensor", lambda e, kt=kt: e.matmul(
            s.ps_s6, s.onesc_b[:], s.e_b[:, kt, :],
            start=(kt == 0), stop=(kt == NKT - 1)), waits=w,
            tag=("smq", c, kt))
        w = [("q2", c, kt)]
        if kt == 0:
            w.append(("rows2", c))
        P.op("tensor", lambda e, kt=kt: e.matmul(
            s.ps_s7, s.onesc_b[:], s.esq_2[:],
            start=(kt == 0), stop=(kt == NKT - 1)), waits=w,
            tag=("smq2", c, kt))

    def emit_p5b(s, c):
        P = s.P
        row = s.row
        P.op("vector", lambda e: e.tensor_copy(row(0), s.ps_s6),
             waits=[("smq", c, NKT - 1), ("wfac", c)], tag=("s14ev", c))
        w = [("smq2", c, NKT - 1)]
        if c > 0:
            w += [("mrs2", c - 1), ("wln2", c - 1)]
        P.op("vector", lambda e: e.tensor_copy(row(1), s.ps_s7),
             waits=w, tag=("s15ev", c))
        P.op("vector", lambda e: e.tensor_scalar(
            row(2), row(0), RD, None, ALU.mult),
            waits=[("s14ev", c), ("ff", c)], tag=("mean2", c))
        P.op("vector", lambda e: e.tensor_scalar(
            row(3), row(1), RD, None, ALU.mult),
            waits=[("s15ev", c), ("onorm0", c)], tag=("eq2", c))
        P.op("vector", lambda e: e.tensor_tensor(
            row(4), row(2), row(2), ALU.mult),
            waits=[("mean2", c), ("dn", c)], tag=("m22", c))
        P.op("vector", lambda e: e.tensor_tensor(
            row(4), row(3), row(4), ALU.subtract),
            waits=[("m22", c), ("eq2", c)], tag=("var2", c))
        P.op("scalar", lambda e: e.activation(
            row(3), row(4), AF.Ln, bias=s.c_eps[0:1, 0:1]),
            waits=[("var2", c), "M_eps"], tag=("lnv2", c))
        P.op("scalar", lambda e: e.activation(
            row(1), row(3), AF.Exp, scale=-0.5),
            waits=[("lnv2", c), ("f1", c)], tag=("rstd2", c))
        P.op("vector", lambda e: e.tensor_tensor(
            row(2), row(2), row(1), ALU.mult),
            waits=[("rstd2", c), ("wfac", c)], tag=("mrs2", c))
        w = [("rstd2", c), ("mrs2", c)]
        w.append(("gln2", c - 1) if c > 0 else "M_rbz")
        P.op("gpsimd", lambda e: e.dma_start(
            out=s.rbscr_d[0:1, 2 * TL:4 * TL],
            in_=s.rows[0:1, 1:3, :].rearrange("o k t -> o (k t)")),
            waits=w, tag=("wln2", c), dma=True, stream="dr3")
        w = [("wln2", c), "M_idx1", ("rc", c, NDT - 1),
             ("tmix", c, NKT - 1)]
        if c > 0:
            w += [("o1", c - 1, NKT - 1), ("outtt", c - 1, NKT - 1)]
        P.op("gpsimd", lambda e: e.indirect_dma_start(
            out=s.rep_sb[:, 0:2, :].rearrange("p k t -> p (k t)"),
            out_offset=None, in_=s.rb2,
            in_offset=bass.IndirectOffsetOnAxis(s.idx1[:, 0:1], axis=0)),
            waits=w, tag=("gln2", c), dma=True, stream="dh3")

    def emit_out_kt(s, c, kt):
        P = s.P
        w = [("gln2", c), ("r2", c, kt), ("smq", c, kt), ("q2", c, kt)]
        if kt >= 1:
            w.append(("outtt", c, kt - 1))
        else:
            w.append(("r2", c, NKT - 1))
        P.op("vector", lambda e, kt=kt: e.tensor_tensor(
            s.tf_2[:], s.e_b[:, kt, :], s.rep_sb[:, 0, :],
            ALU.mult), waits=w, tag=("o1", c, kt))
        w = [("o1", c, kt)]
        if kt >= 1:
            w.append(("st_out", c, kt - 1))
        elif c > 0:
            w.append(("st_out", c - 1, NKT - 1))
        P.op("vector", lambda e, kt=kt: e.tensor_tensor(
            s.out_2[:], s.tf_2[:], s.rep_sb[:, 1, :],
            ALU.subtract), waits=w, tag=("outtt", c, kt))
        P.op("scalar", lambda e, kt=kt: e.dma_start(
            out=s.out_d[kt * 128:(kt + 1) * 128, c * TL:(c + 1) * TL],
            in_=s.out_2[:]),
            waits=[("outtt", c, kt)], tag=("st_out", c, kt), dma=True,
            stream="dsA" if kt % 2 == 0 else "dsB")


_NC = None


def kernel(L0_attn_x, L0_attn_x_proj_w, L0_attn_dt_proj_w, L0_attn_dt_proj_b,
           L0_attn_A_log, L0_attn_D_param, L0_ffn_weight, L0_ffn_scale):
    global _NC
    x = np.asarray(L0_attn_x, np.float32)
    xpw = np.asarray(L0_attn_x_proj_w, np.float32)
    dtw = np.asarray(L0_attn_dt_proj_w, np.float32)
    dtb = np.asarray(L0_attn_dt_proj_b, np.float32)
    alog = np.asarray(L0_attn_A_log, np.float32)
    dp = np.asarray(L0_attn_D_param, np.float32)
    wf = np.asarray(L0_ffn_weight, np.float32)
    scl = np.asarray(L0_ffn_scale, np.float32)

    xpw_r = np.ascontiguousarray(
        xpw.T.reshape(NDT, 128, 160).transpose(1, 0, 2))
    dtw_r = np.ascontiguousarray(dtw.T)
    dtb_r = np.ascontiguousarray(dtb.reshape(NDT, 128).T)
    alog_r = np.ascontiguousarray(
        alog.reshape(NDT, 128, NST).transpose(1, 0, 2).reshape(128, NDT * NST))
    dp_r = np.ascontiguousarray(dp.reshape(NDT, 128).T)
    wsw = np.ascontiguousarray(
        wf.reshape(NDT, 128, NKT, 128).transpose(2, 1, 0, 3))
    scl_r = np.ascontiguousarray(scl.reshape(NKT, 128).T)
    eye = np.eye(128, dtype=ml_dtypes.bfloat16)
    onesc = np.ones((128, 1), dtype=ml_dtypes.bfloat16)

    if _NC is None:
        _NC = K().build()
    in_maps = []
    for cc in range(8):
        in_maps.append({
            "xT": np.ascontiguousarray(x[cc].T), "xpw": xpw_r, "dtw": dtw_r,
            "dtb": dtb_r, "alog": alog_r, "dp": dp_r, "wsw": wsw,
            "scl": scl_r, "eye": eye, "onesc": onesc,
        })
    res = run_bass_kernel_spmd(_NC, in_maps, core_ids=list(range(8)))
    out = np.stack([np.asarray(r["out"], np.float32).T
                    for r in res.results], axis=0)
    return np.ascontiguousarray(out)


# version 18
# speedup vs baseline: 1.0937x; 1.0937x over previous
import numpy as np
import ml_dtypes
import concourse.bass as bass
import concourse.mybir as mybir
from concourse.bass_utils import run_bass_kernel_spmd
from contextlib import ExitStack

F32 = mybir.dt.float32
BF16 = mybir.dt.bfloat16
I32 = mybir.dt.int32
AF = mybir.ActivationFunctionType
ALU = mybir.AluOpType

L = 2048
DM = 2048
NST = 16
TL = 512
NCH = L // TL
NDT = DM // 128
NKT = 16
QN = 4
NQ = NST // QN
EPS_LN = 1e-5
EPS_F = 1e-7
MAXN = 0.99
RD = 1.0 / DM

ENGS = ("sync", "gpsimd", "tensor", "scalar", "vector")
CSEM = {"sync": "cy", "gpsimd": "cg", "tensor": "ct", "scalar": "cs",
        "vector": "cv"}
BARRIER_STREAMS = {"dy_const", "dgc", "dgw"}
SAME_ENGINE_SYNC = True


class Prog:

    def __init__(self):
        self.steps = {e: [] for e in ENGS}
        self.waited = set()
        self.tag_step = {}

    def op(self, eng, fn, waits=(), tag=None, dma=False, stream=None):
        if tag is not None:
            assert tag not in self.tag_step, f"dup tag {tag}"
        assert (stream is not None) == bool(dma), f"dma needs stream: {tag}"
        st = {"eng": eng, "fn": fn, "waits": tuple(waits), "tag": tag,
              "dma": dma, "stream": stream}
        self.steps[eng].append(st)
        if tag is not None:
            self.tag_step[tag] = st
        for w in waits:
            self.waited.add(w)

    def finalize(self):
        for w in self.waited:
            assert w in self.tag_step, f"wait on unknown tag {w}"
        cnt = {}
        self.tagval = {}
        barrier_tags = {}
        for eng in ENGS:
            for st in self.steps[eng]:
                if st["dma"]:
                    st["inc"] = True
                    sem = st["stream"]
                    cnt[sem] = cnt.get(sem, 0) + 16
                    if st["tag"] is not None:
                        self.tagval[st["tag"]] = (sem, cnt[sem])
                        barrier_tags.setdefault(sem, []).append(st["tag"])
                else:
                    need = st["tag"] is not None and st["tag"] in self.waited
                    st["inc"] = need
                    if need:
                        sem = CSEM[eng]
                        cnt[sem] = cnt.get(sem, 0) + 1
                        self.tagval[st["tag"]] = (sem, cnt[sem])
        for s in BARRIER_STREAMS:
            for t in barrier_tags.get(s, ()):
                self.tagval[t] = (s, cnt[s])
        self._check_cycles()
        return sorted(set(CSEM.values()) | set(cnt))

    def _check_cycles(self):
        n = 0
        nid = {}
        for eng in ENGS:
            for st in self.steps[eng]:
                nid[id(st)] = n
                n += 1
                if st["dma"]:
                    n += 1
        adj = [[] for _ in range(n)]
        indeg = [0] * n

        def edge(a, b):
            adj[a].append(b)
            indeg[b] += 1

        stream_steps = {}
        for eng in ENGS:
            for st in self.steps[eng]:
                if st["dma"]:
                    stream_steps.setdefault(st["stream"], []).append(st)
        for eng in ENGS:
            prev = None
            for st in self.steps[eng]:
                a = nid[id(st)]
                if prev is not None:
                    edge(prev, a)
                prev = a
                if st["dma"]:
                    edge(a, a + 1)
                for w in st["waits"]:
                    src = self.tag_step[w]
                    if src["dma"] and src["stream"] in BARRIER_STREAMS:
                        for s2 in stream_steps[src["stream"]]:
                            edge(nid[id(s2)] + 1, a)
                    else:
                        s = nid[id(src)]
                        if src["dma"]:
                            s += 1
                        edge(s, a)
        from collections import deque
        q = deque(i for i in range(n) if indeg[i] == 0)
        seen = 0
        while q:
            u = q.popleft()
            seen += 1
            for v in adj[u]:
                indeg[v] -= 1
                if indeg[v] == 0:
                    q.append(v)
        assert seen == n, f"DEADLOCK: dependency cycle ({n - seen} stuck)"

    def emit_engine(self, eng_handle, eng_name, sems):
        hw = {}
        myc = CSEM[eng_name]
        for st in self.steps[eng_name]:
            for w in st["waits"]:
                sem, val = self.tagval[w]
                if not SAME_ENGINE_SYNC and sem == myc:
                    continue
                if hw.get(sem, -1) >= val:
                    continue
                eng_handle.wait_ge(sems[sem], val)
                hw[sem] = val
            ins = st["fn"](eng_handle)
            if st["inc"]:
                sem = st["stream"] if st["dma"] else myc
                ins.then_inc(sems[sem], 16 if st["dma"] else 1)


class K:

    def __init__(self):
        self.P = Prog()

    def build(self):
        nc = bass.Bass("TRN2", target_bir_lowering=False)
        self.nc = nc
        s = self

        s.xT_d = nc.dram_tensor("xT", [DM, L], F32, kind="ExternalInput").ap()
        s.xpw_d = nc.dram_tensor("xpw", [128, NDT, 160], F32,
                                 kind="ExternalInput").ap()
        s.dtw_d = nc.dram_tensor("dtw", [128, DM], F32,
                                 kind="ExternalInput").ap()
        s.dtb_d = nc.dram_tensor("dtb", [128, NDT], F32,
                                 kind="ExternalInput").ap()
        s.alog_d = nc.dram_tensor("alog", [128, NDT * NST], F32,
                                  kind="ExternalInput").ap()
        s.dp_d = nc.dram_tensor("dp", [128, NDT], F32,
                                kind="ExternalInput").ap()
        s.wsw_d = nc.dram_tensor("wsw", [NKT, 128, NDT, 128], F32,
                                 kind="ExternalInput").ap()
        s.scl_d = nc.dram_tensor("scl", [128, NKT], F32,
                                 kind="ExternalInput").ap()
        s.eye_d = nc.dram_tensor("eye", [128, 128], BF16,
                                 kind="ExternalInput").ap()
        s.onesc_d = nc.dram_tensor("onesc", [128, 1], BF16,
                                   kind="ExternalInput").ap()
        s.out_d = nc.dram_tensor("out", [DM, L], F32,
                                 kind="ExternalOutput").ap()
        s.wbf_d = nc.dram_tensor("wbf", [NKT, 128, NDT, 128], BF16,
                                 kind="Internal").ap()
        s.bcscr_d = nc.dram_tensor("bcscr", [32, TL], BF16, kind="Internal").ap()
        s.bcs2 = s.bcscr_d.rearrange("(g n) t -> g (n t)", g=2)
        s.rbscr_d = nc.dram_tensor("rbscr", [1, 4 * TL], BF16,
                                   kind="Internal").ap()
        s.rb4 = s.rbscr_d.rearrange("o (r t) -> (o r) t", r=4)
        s.rb2 = s.rbscr_d.rearrange("o (r t) -> (o r) t", r=2)

        with ExitStack() as ctx:
            def sb(name, shape, dt):
                return ctx.enter_context(nc.sbuf_tensor(name, shape, dt))

            s.c_xpw = sb("c_xpw", [128, NDT, 160], BF16)
            s.c_dtw = sb("c_dtw", [128, DM], BF16)
            s.c_dtb = sb("c_dtb", [128, NDT], F32)
            s.c_A = sb("c_A", [128, NDT * NST], F32)
            s.c_dp = sb("c_dp", [128, NDT], F32)
            s.c_D1 = sb("c_D1", [128, NDT], F32)
            s.c_diag = sb("c_diag", [128, NDT, 128], BF16)
            s.c_scl = sb("c_scl", [128, NKT], F32)
            s.eye_b = sb("eye_b", [128, 128], BF16)
            s.onesc_b = sb("onesc_b", [128, 1], BF16)
            s.idx0 = sb("idx0", [128, 1], I32)
            s.idx1 = sb("idx1", [128, 1], I32)
            s.c_eps = sb("c_eps", [128, 2], F32)
            s.state_sb = sb("state_sb", [128, NDT, NST], BF16)

            s.x_b = sb("x_b", [128, NDT, TL], BF16)
            s.delta_b = sb("delta_b", [128, NDT, TL], BF16)
            s.du_b = sb("du_b", [128, TL], BF16)
            s.bc32 = sb("bc32", [32, TL], BF16)
            s.t1_2 = sb("t1_2", [128, TL], BF16)
            s.u_2 = sb("u_2", [128, TL], BF16)
            s.B_rep = sb("B_rep", [128, NST, TL], BF16)
            s.C_rep = sb("C_rep", [128, NST, TL], BF16)
            s.dae_x = sb("dae_x", [128, 2, QN, TL + 1], F32)
            s.bbh = sb("bbh", [128, 2, QN, TL + 1], BF16)
            s.g_x = sb("g_x", [128, 4, QN, TL], BF16)
            s.r_b = sb("r_b", [128, NDT, TL], BF16)
            s.rsq_2 = sb("rsq_2", [128, TL], BF16)
            s.rc_b = sb("rc_b", [128, NDT, TL], BF16)
            s.w_sl = sb("w_sl", [128, 2, NDT, 128], BF16)
            s.e_b = sb("e_b", [128, NKT, TL], BF16)
            s.esq_2 = sb("esq_2", [128, TL], BF16)
            s.tf_2 = sb("tf_2", [128, 2, TL], BF16)
            s.out_2 = sb("out_2", [128, 2, TL], F32)
            s.rows = sb("rows", [1, 7, TL], F32)
            s.rep_sb = sb("rep_sb", [128, 3, TL], BF16)

            pb1 = ctx.enter_context(nc.psum_tensor("pb1", [128, 2048], F32))
            pb2 = ctx.enter_context(nc.psum_tensor("pb2", [128, 2048], F32))
            s.ps_du = pb1[:, 0:512]
            s.ps_bc = pb1[0:32, 512:1024]
            s.ps_dpre = lambda p: pb1[:, p * 512:(p + 1) * 512]
            s.ps_f = lambda p: pb1[:, 1024 + p * 512:1536 + p * 512]
            s.ps_r = lambda p: pb2[:, p * 512:(p + 1) * 512]
            s.ps_s6 = pb2[0:1, 1024:1536]
            s.ps_s7 = pb2[0:1, 1536:2048]

            s.row = lambda i: s.rows[0:1, i, :]

            s.emit_all()

            sems_needed = s.P.finalize()
            with ExitStack() as sctx:
                sems = {nm: sctx.enter_context(nc.semaphore(nm))
                        for nm in sems_needed}
                block = sctx.enter_context(nc.Block())
                for name in ENGS:
                    getattr(block, name)(
                        lambda eng, name=name: s.P.emit_engine(eng, name, sems))
        return nc

    def emit_all(s):
        s.emit_setup()
        s.emit_p1(0)
        for dt in range(NDT):
            s.emit_p2_core(0, dt)
            if dt >= 1:
                s.emit_p2_tail(0, dt - 1)
            if dt >= 2:
                s.emit_rev(0, dt - 2)
        s.emit_p2_tail(0, NDT - 1)
        s.emit_rev(0, NDT - 2)
        s.emit_rev(0, NDT - 1)
        s.emit_stats(0)
        s.emit_p3_rows(0)
        s.emit_p3_rc(0)
        s.emit_p1(1)
        for c in range(NCH):
            nxt = c + 1 if c + 1 < NCH else None

            def wv(i, base):
                dt = base + i
                s.emit_p2_core(nxt, dt)
                if dt >= 1:
                    s.emit_p2_tail(nxt, dt - 1)
                if dt >= 2:
                    s.emit_rev(nxt, dt - 2)
            for i in range(8):
                if nxt is not None:
                    wv(i, 0)
                s.emit_p4_kt(c, 2 * i)
                s.emit_p4_kt(c, 2 * i + 1)
            s.emit_p5a(c)
            for i in range(8):
                if nxt is not None:
                    wv(i, 8)
                s.emit_p5_kt(c, 2 * i)
                s.emit_p5_kt(c, 2 * i + 1)
            if nxt is not None:
                s.emit_p2_tail(nxt, NDT - 1)
                s.emit_rev(nxt, NDT - 2)
                s.emit_rev(nxt, NDT - 1)
            if c + 2 < NCH:
                s.emit_p1_dma(c + 2)
            s.emit_p5b(c)
            if nxt is not None:
                s.emit_stats(nxt)
                s.emit_p3_rows(nxt)
            if c + 2 < NCH:
                s.emit_p1_comp(c + 2)
            for kt in range(NKT // 2):
                s.emit_out_kt(c, kt)
            if nxt is not None:
                s.emit_p3_rc(nxt)
            for kt in range(NKT // 2, NKT):
                s.emit_out_kt(c, kt)
        last = [("st_out", NCH - 1, NKT - 2), ("st_out", NCH - 1, NKT - 1)]
        s.P.op("vector", lambda e: e.engine_nop(), waits=last, tag="FINAL")

    def emit_setup(s):
        P = s.P
        P.op("sync", lambda e: e.dma_start(out=s.c_dtb[:], in_=s.dtb_d),
             dma=True, tag="L_dtb", stream="dy_const")
        P.op("sync", lambda e: e.dma_start(out=s.c_A[:], in_=s.alog_d),
             dma=True, tag="L_alog", stream="dy_const")
        P.op("sync", lambda e: e.dma_start(out=s.c_dp[:], in_=s.dp_d),
             dma=True, tag="L_dp", stream="dy_const")
        P.op("sync", lambda e: e.dma_start(out=s.c_scl[:], in_=s.scl_d),
             dma=True, tag="L_scl", stream="dy_const")
        P.op("sync", lambda e: e.dma_start(out=s.eye_b[:], in_=s.eye_d),
             dma=True, tag="L_eye", stream="dy_const")
        P.op("sync", lambda e: e.dma_start(out=s.onesc_b[:], in_=s.onesc_d),
             dma=True, tag="L_onesc", stream="dy_const")
        P.op("gpsimd", lambda e: e.dma_start(out=s.c_xpw[:], in_=s.xpw_d),
             dma=True, tag="L_xpw", stream="dgc")
        P.op("gpsimd", lambda e: e.dma_start(out=s.c_dtw[:], in_=s.dtw_d),
             dma=True, tag="L_dtw", stream="dgc")
        for s4 in range(4):
            P.op("gpsimd", lambda e, s4=s4: e.dma_start(
                out=s.wbf_d[s4 * 4:(s4 + 1) * 4],
                in_=s.wsw_d[s4 * 4:(s4 + 1) * 4]),
                dma=True, tag=("wcast", s4), stream="dgw")
        P.op("gpsimd", lambda e: e.memset(s.state_sb[:], 0.0), tag="M_state")
        P.op("gpsimd", lambda e: e.memset(s.dae_x[:, 0, :, 0:1], 0.0),
             tag="M_dae0")
        P.op("gpsimd", lambda e: e.memset(s.dae_x[:, 1, :, 0:1], 0.0),
             tag="M_dae1")
        P.op("gpsimd", lambda e: e.memset(s.idx0[:], 0), tag="M_idx0")
        P.op("gpsimd", lambda e: e.memset(s.idx1[:], 1), tag="M_idx1")
        P.op("gpsimd", lambda e: e.memset(s.c_eps[:, 0:1], EPS_LN),
             tag="M_eps")
        P.op("gpsimd", lambda e: e.memset(
            s.rows[0:1, 0:4, :].rearrange("o k t -> o (k t)"), 0.0),
            tag="M_rb0")
        P.op("gpsimd", lambda e: e.dma_start(
            out=s.rbscr_d,
            in_=s.rows[0:1, 0:4, :].rearrange("o k t -> o (k t)")),
            waits=["M_rb0"], tag="M_rbz", dma=True, stream="dgz")
        P.op("scalar", lambda e: e.activation(s.c_A[:], s.c_A[:], AF.Exp),
             waits=["L_alog"], tag="S_Aexp")
        P.op("vector", lambda e: e.tensor_scalar(
            s.c_A[:], s.c_A[:], -1.0, None, ALU.mult), waits=["S_Aexp"],
            tag="S_Aneg")
        P.op("scalar", lambda e: e.add(s.c_D1[:], s.c_dp[:], 1.0),
             waits=["L_dp"], tag="S_D1")
        for dt in range(NDT):
            P.op("vector", lambda e, dt=dt: e.tensor_scalar(
                s.c_diag[:, dt, :], s.eye_b[:], s.c_D1[:, dt:dt + 1], None,
                ALU.mult), waits=["L_eye", "S_D1"], tag=("S_diag", dt))

    def emit_p1(s, c):
        s.emit_p1_dma(c)
        s.emit_p1_comp(c)

    def emit_p1_dma(s, c):
        P = s.P
        xin = s.xT_d.rearrange("(dt p) l -> p dt l", p=128)[
            :, :, c * TL:(c + 1) * TL]
        xw = []
        if c > 0:
            xw = [("diagmm", c - 1, NDT - 1), ("u", c - 1, NDT - 1)]
        P.op("gpsimd", lambda e: e.dma_start(out=s.x_b[:], in_=xin),
             waits=xw, tag=("xload", c), dma=True, stream="dgx")

    def emit_p1_comp(s, c):
        P = s.P
        for dt in range(NDT):
            w = [("xload", c), "L_xpw"]
            if c > 0 and dt == 0:
                w += [("sp1", c - 1, NDT - 2)]
            P.op("tensor", lambda e, dt=dt: e.matmul(
                s.ps_du, s.c_xpw[:, dt, 0:128], s.x_b[:, dt, :],
                start=(dt == 0), stop=(dt == NDT - 1)), waits=w,
                tag=("xdbl", c, dt))
            w2 = []
            if c > 0 and dt == 0:
                w2 = [("sp1", c - 1, NDT - 1)]
            P.op("tensor", lambda e, dt=dt: e.matmul(
                s.ps_bc, s.c_xpw[:, dt, 128:160], s.x_b[:, dt, :],
                start=(dt == 0), stop=(dt == NDT - 1)), waits=w2,
                tag=("xdblbc", c, dt))
        w = [("xdbl", c, NDT - 1)]
        if c > 0:
            w.append(("dpre", c - 1, NDT - 1))
        P.op("scalar", lambda e: e.activation(s.du_b[:], s.ps_du, AF.Copy),
             waits=w, tag=("du_ev", c))
        P.op("scalar", lambda e: e.activation(s.bc32[:], s.ps_bc, AF.Copy),
             waits=[("xdblbc", c, NDT - 1)], tag=("bc_ev", c))
        scw = [("bc_ev", c)]
        if c > 0:
            scw += [("gatB", c - 1), ("gatC", c - 1)]
        P.op("gpsimd", lambda e: e.dma_start(out=s.bcscr_d, in_=s.bc32[:]),
             waits=scw, tag=("bscr", c), dma=True, stream="dgs")
        gw = [("bscr", c), "M_idx0"]
        if c > 0:
            gw += [("bb", c - 1, NDT - 1, NQ - 1),
                   ("bb", c - 1, NDT - 1, 0)]
        P.op("gpsimd", lambda e: e.indirect_dma_start(
            out=s.B_rep[:].rearrange("p n t -> p (n t)"), out_offset=None,
            in_=s.bcs2,
            in_offset=bass.IndirectOffsetOnAxis(s.idx0[:, 0:1], axis=0)),
            waits=gw, tag=("gatB", c), dma=True, stream="dgb")
        gw2 = [("bscr", c), "M_idx1"]
        if c > 0:
            gw2 += [("cmul", c - 1, NDT - 1, NQ - 1),
                    ("cmul", c - 1, NDT - 1, 0)]
        P.op("gpsimd", lambda e: e.indirect_dma_start(
            out=s.C_rep[:].rearrange("p n t -> p (n t)"), out_offset=None,
            in_=s.bcs2,
            in_offset=bass.IndirectOffsetOnAxis(s.idx1[:, 0:1], axis=0)),
            waits=gw2, tag=("gatC", c), dma=True, stream="dgg")
        for dt in range(NDT):
            par = dt % 2
            w = [("du_ev", c), "L_dtw"]
            if dt == 1:
                w.append(("bc_ev", c))
            if dt >= 2:
                w.append(("sp1", c, dt - 2))
            P.op("tensor", lambda e, dt=dt, par=par: e.matmul(
                s.ps_dpre(par), s.c_dtw[:, dt * 128:(dt + 1) * 128],
                s.du_b[:], start=True, stop=True), waits=w,
                tag=("dpre", c, dt))
            w = [("dpre", c, dt), "L_dtb"]
            if dt >= 1:
                w.append(("sp2", c, dt - 1))
            P.op("scalar", lambda e, dt=dt, par=par: e.activation(
                s.t1_2[:], s.ps_dpre(par), AF.Exp,
                bias=s.c_dtb[:, dt:dt + 1]), waits=w, tag=("sp1", c, dt))
            w = [("sp1", c, dt)]
            if c > 0:
                w.append(("u", c - 1, dt))
            P.op("scalar", lambda e, dt=dt, par=par: e.activation(
                s.delta_b[:, dt, :], s.t1_2[:], AF.Ln, bias=1.0),
                waits=w, tag=("sp2", c, dt))

    def emit_p2_core(s, c, dt):
        P = s.P
        w = [("sp2", c, dt), ("xload", c)]
        if dt >= 1:
            w += [("bb", c, dt - 1, NQ - 1), ("bb", c, dt - 1, 0)]
        elif c > 0:
            w += [("bb", c - 1, NDT - 1, NQ - 1),
                  ("bb", c - 1, NDT - 1, 0)]
        P.op("vector", lambda e, dt=dt: e.tensor_tensor(
            s.u_2[:], s.delta_b[:, dt, :], s.x_b[:, dt, :],
            ALU.mult), waits=w, tag=("u", c, dt))

        for q in range(NQ):
            gidx = (c * NDT + dt) * NQ + q
            par = gidx % 2
            pg = gidx - 2
            pc, pdt, pq = (pg // (NDT * NQ), (pg // NQ) % NDT, pg % NQ)
            prev = pg >= 0
            bbeng = "gpsimd" if q == 0 else "vector"

            for j in range(QN):
                n = q * QN + j
                w = [("sp2", c, dt), "S_Aneg", "M_dae0", "M_dae1"]
                if prev and j == 0:
                    w.append(("scan", pc, pdt, pq))
                P.op("scalar", lambda e, dt=dt, par=par, j=j, n=n:
                     e.activation(s.dae_x[:, par, j, 1:TL + 1],
                                  s.delta_b[:, dt, :], AF.Exp,
                                  scale=s.c_A[:, dt * NST + n:
                                              dt * NST + n + 1]),
                     waits=w, tag=("dae", c, dt, q, j))

            w = [("u", c, dt), ("gatB", c)]
            if prev:
                w += [("cmul", pc, pdt, pq), ("cout", pc, pdt, pq)]
            P.op(bbeng, lambda e, dt=dt, q=q, par=par: e.tensor_tensor(
                s.bbh[:, par, :, 1:TL + 1],
                s.u_2[:].unsqueeze(1).broadcast_to([128, QN, TL]),
                s.B_rep[:, q * QN:(q + 1) * QN, :], ALU.mult),
                waits=w, tag=("bb", c, dt, q))

            w = ["M_state"] if c == 0 else [("cout", c - 1, dt, q)]
            if prev:
                w.append(("scan", pc, pdt, pq))
            P.op("vector", lambda e, dt=dt, q=q, par=par: e.tensor_copy(
                s.bbh[:, par, :, 0:1],
                s.state_sb[:, dt, q * QN:(q + 1) * QN].unsqueeze(2)),
                waits=w, tag=("cin", c, dt, q))

            w = [("dae", c, dt, q, QN - 1), ("bb", c, dt, q),
                 ("cin", c, dt, q)]
            P.op("vector", lambda e, par=par: e.tensor_tensor_scan(
                s.bbh[:, par].rearrange("p q t -> p (q t)"),
                s.dae_x[:, par].rearrange("p q t -> p (q t)"),
                s.bbh[:, par].rearrange("p q t -> p (q t)"),
                0.0, ALU.mult, ALU.add),
                waits=w, tag=("scan", c, dt, q))

            P.op("vector", lambda e, dt=dt, q=q, par=par: e.tensor_copy(
                s.state_sb[:, dt, q * QN:(q + 1) * QN].unsqueeze(2),
                s.bbh[:, par, :, TL:TL + 1]),
                waits=[("scan", c, dt, q)], tag=("cout", c, dt, q))

            w = [("scan", c, dt, q), ("gatC", c)]
            g4 = gidx - 4
            if g4 >= 0:
                w.append(("idmm", g4 // (NDT * NQ), (g4 // NQ) % NDT,
                          g4 % NQ, QN - 1))
            gpar = gidx % 4
            P.op("vector", lambda e, dt=dt, q=q, par=par, gpar=gpar:
                 e.tensor_tensor(
                s.g_x[:, gpar, :, :], s.bbh[:, par, :, 1:TL + 1],
                s.C_rep[:, q * QN:(q + 1) * QN, :], ALU.mult),
                waits=w, tag=("cmul", c, dt, q))

    def emit_p2_tail(s, c, dt):
        P = s.P
        for q in range(NQ):
            gpar = ((c * NDT + dt) * NQ + q) % 4
            for j in range(QN):
                w = [("cmul", c, dt, q), "L_eye"]
                if q == 0 and j == 0 and c * NDT + dt >= 2:
                    rdt, rc_ = dt - 2, c
                    if rdt < 0:
                        rdt += NDT
                        rc_ -= 1
                    w.append(("rev", rc_, rdt))
                P.op("tensor", lambda e, q=q, j=j, gpar=gpar, dt=dt: e.matmul(
                    s.ps_r(dt % 2), s.eye_b[:], s.g_x[:, gpar, j, :],
                    start=(q == 0 and j == 0), stop=False),
                    waits=w, tag=("idmm", c, dt, q, j))

        P.op("tensor", lambda e, dt=dt: e.matmul(
            s.ps_r(dt % 2), s.c_diag[:, dt, :], s.x_b[:, dt, :],
            start=False, stop=True),
            waits=[("xload", c), ("S_diag", dt)], tag=("diagmm", c, dt))

    def emit_rev(s, c, dt):
        P = s.P
        w = [("diagmm", c, dt)]
        if c > 0:
            w.append(("rc", c - 1, dt))
        P.op("scalar", lambda e, dt=dt: e.activation(
            s.r_b[:, dt, :], s.ps_r(dt % 2), AF.Copy), waits=w,
            tag=("rev", c, dt))

    def emit_stats(s, c):
        P = s.P
        for dt in range(NDT):
            w = [("rev", c, dt), "L_onesc"]
            if dt == 0 and c > 0:
                w.append(("s14ev", c - 1))
            P.op("tensor", lambda e, dt=dt: e.matmul(
                s.ps_s6, s.onesc_b[:], s.r_b[:, dt, :],
                start=(dt == 0), stop=(dt == NDT - 1)), waits=w,
                tag=("smr", c, dt))
            w = [("rev", c, dt)]
            if dt >= 1:
                w.append(("smr2", c, dt - 1))
            elif c > 0:
                w.append(("smr2", c - 1, NDT - 1))
            P.op("scalar", lambda e, dt=dt: e.activation(
                s.rsq_2[:], s.r_b[:, dt, :], AF.Square),
                waits=w, tag=("rsq", c, dt))
            w = [("rsq", c, dt)]
            if dt == 0 and c > 0:
                w.append(("s15ev", c - 1))
            P.op("tensor", lambda e, dt=dt: e.matmul(
                s.ps_s7, s.onesc_b[:], s.rsq_2[:],
                start=(dt == 0), stop=(dt == NDT - 1)), waits=w,
                tag=("smr2", c, dt))

    def emit_p3_rows(s, c):
        P = s.P
        row = s.row
        w = [("smr", c, NDT - 1)]
        if c == 0:
            w.append("M_rbz")
        P.op("vector", lambda e: e.tensor_copy(row(0), s.ps_s6),
             waits=w, tag=("rows1", c))
        P.op("vector", lambda e: e.tensor_copy(row(1), s.ps_s7),
             waits=[("smr2", c, NDT - 1)], tag=("rows2", c))
        P.op("vector", lambda e: e.tensor_scalar(
            row(2), row(0), RD, None, ALU.mult), waits=[("rows1", c)],
            tag=("mean", c))
        P.op("vector", lambda e: e.tensor_scalar(
            row(3), row(1), RD, None, ALU.mult), waits=[("rows2", c)],
            tag=("er2", c))
        P.op("vector", lambda e: e.tensor_tensor(
            row(4), row(2), row(2), ALU.mult), waits=[("mean", c)],
            tag=("m2", c))
        P.op("vector", lambda e: e.tensor_tensor(
            row(4), row(3), row(4), ALU.subtract),
            waits=[("m2", c), ("er2", c)], tag=("var", c))
        P.op("scalar", lambda e: e.activation(
            row(3), row(4), AF.Ln, bias=s.c_eps[0:1, 0:1]),
            waits=[("var", c), "M_eps"], tag=("lnv", c))
        w = [("lnv", c)]
        if c > 0:
            w.append(("wln2", c - 1))
        P.op("scalar", lambda e: e.activation(
            row(5), row(3), AF.Exp, scale=-0.5),
            waits=w, tag=("rstd", c))
        w = [("mean", c)]
        w.append(("gmean", c - 1) if c > 0 else "M_rbz")
        P.op("gpsimd", lambda e: e.dma_start(
            out=s.rbscr_d[0:1, 0:TL], in_=row(2)), waits=w,
            tag=("wmean", c), dma=True, stream="dr1")
        w = [("wmean", c), "M_idx0"]
        if c > 0:
            w.append(("rc", c - 1, NDT - 1))
        P.op("gpsimd", lambda e: e.indirect_dma_start(
            out=s.rep_sb[:, 2, :], out_offset=None,
            in_=s.rb4,
            in_offset=bass.IndirectOffsetOnAxis(s.idx0[:, 0:1], axis=0)),
            waits=w, tag=("gmean", c), dma=True, stream="dh1")

    def emit_p3_rc(s, c):
        P = s.P
        for dt in range(NDT):
            w = [("gmean", c), ("rev", c, dt)]
            if c > 0:
                w.append(("ffnmm", c - 1, NKT - 1, dt))
            P.op("vector", lambda e, dt=dt: e.tensor_tensor(
                s.rc_b[:, dt, :], s.r_b[:, dt, :], s.rep_sb[:, 2, :],
                ALU.subtract), waits=w, tag=("rc", c, dt))

    def emit_p4_kt(s, c, kt):
        P = s.P
        w = [("wcast", kt // 4)]
        gk = c * NKT + kt
        if gk >= 2:
            pk, pc2 = kt - 2, c
            if pk < 0:
                pk += NKT
                pc2 -= 1
            w.append(("ffnmm", pc2, pk, NDT - 1))
        P.op("sync", lambda e, kt=kt: e.dma_start(
            out=s.w_sl[:, kt % 2, :, :], in_=s.wbf_d[kt]), waits=w,
            tag=("wld", c, kt), dma=True,
            stream="dwA" if kt % 2 == 0 else "dwB")
        for dt in range(NDT):
            w = [("wld", c, kt), ("rc", c, dt)]
            if dt == 0:
                if gk >= 2:
                    w.append(("scl_ev", pc2, pk))
                if kt == 0:
                    w += [("sp1", c, NDT - 1), ("sp1", c, NDT - 2)]
            P.op("tensor", lambda e, kt=kt, dt=dt: e.matmul(
                s.ps_f(kt % 2), s.w_sl[:, kt % 2, dt, :], s.rc_b[:, dt, :],
                start=(dt == 0), stop=(dt == NDT - 1)), waits=w,
                tag=("ffnmm", c, kt, dt))
        w = [("ffnmm", c, kt, NDT - 1), "L_scl"]
        if c > 0:
            w += [("o1", c - 1, kt), ("smq", c - 1, kt), ("q2", c - 1, kt)]
        P.op("scalar", lambda e, kt=kt: e.activation(
            s.e_b[:, kt, :], s.ps_f(kt % 2), AF.Copy,
            scale=s.c_scl[:, kt:kt + 1]), waits=w, tag=("scl_ev", c, kt))
        w = [("scl_ev", c, kt)]
        if kt >= 1:
            w.append(("sme", c, kt - 1))
        elif c > 0:
            w.append(("smq2", c - 1, NKT - 1))
        P.op("scalar", lambda e, kt=kt: e.activation(
            s.esq_2[:], s.e_b[:, kt, :], AF.Square), waits=w,
            tag=("esq", c, kt))
        w = [("esq", c, kt)]
        if kt == 0:
            w.append(("rows1", c))
        P.op("tensor", lambda e, kt=kt: e.matmul(
            s.ps_s6, s.onesc_b[:], s.esq_2[:],
            start=(kt == 0), stop=(kt == NKT - 1)), waits=w,
            tag=("sme", c, kt))

    def emit_p5a(s, c):
        P = s.P
        row = s.row
        P.op("vector", lambda e: e.tensor_copy(row(0), s.ps_s6),
             waits=[("sme", c, NKT - 1)], tag=("s9ev", c))
        P.op("scalar", lambda e: e.activation(row(1), row(0), AF.Ln),
             waits=[("s9ev", c)], tag=("lnn", c))
        P.op("scalar", lambda e: e.activation(row(2), row(1), AF.Exp,
                                              scale=0.5),
             waits=[("lnn", c), ("wmean", c)], tag=("qn", c))
        P.op("vector", lambda e: e.tensor_tensor(
            row(3), row(5), row(2), ALU.mult),
            waits=[("qn", c), ("rstd", c)], tag=("vnr", c))
        P.op("vector", lambda e: e.tensor_scalar(
            row(4), row(3), EPS_F, None, ALU.max), waits=[("vnr", c)],
            tag=("vn", c))
        P.op("vector", lambda e: e.tensor_scalar(
            row(6), row(4), 15.0, None, ALU.min), waits=[("vn", c)],
            tag=("vnc", c))
        P.op("scalar", lambda e: e.activation(row(6), row(6), AF.Exp,
                                              scale=2.0),
             waits=[("vnc", c)], tag=("te", c))
        P.op("vector", lambda e: e.tensor_scalar(
            row(6), row(6), 1.0, None, ALU.add), waits=[("te", c)],
            tag=("te1", c))
        P.op("scalar", lambda e: e.activation(row(0), row(6), AF.Ln),
             waits=[("te1", c)], tag=("te2a", c))
        P.op("scalar", lambda e: e.activation(row(0), row(0), AF.Exp,
                                              scale=-1.0),
             waits=[("te2a", c)], tag=("te2", c))
        P.op("vector", lambda e: e.tensor_scalar(
            row(6), row(0), -2.0, 1.0, ALU.mult, ALU.add),
            waits=[("te2", c)], tag=("tanh", c))
        P.op("vector", lambda e: e.tensor_scalar(
            row(0), row(4), EPS_F, None, ALU.add),
            waits=[("vn", c), ("tanh", c)], tag=("dn", c))
        P.op("scalar", lambda e: e.activation(row(1), row(0), AF.Ln),
             waits=[("dn", c)], tag=("dnra", c))
        P.op("scalar", lambda e: e.activation(row(1), row(1), AF.Exp,
                                              scale=-1.0),
             waits=[("dnra", c)], tag=("dnr", c))
        P.op("vector", lambda e: e.tensor_tensor(
            row(1), row(6), row(1), ALU.mult),
            waits=[("tanh", c), ("dnr", c)], tag=("f1", c))
        P.op("vector", lambda e: e.tensor_tensor(
            row(6), row(1), row(3), ALU.mult), waits=[("f1", c)],
            tag=("onorm0", c))
        P.op("vector", lambda e: e.tensor_scalar(
            row(6), row(6), EPS_F, None, ALU.max), waits=[("onorm0", c)],
            tag=("onorm", c))
        P.op("scalar", lambda e: e.activation(row(2), row(6), AF.Ln),
             waits=[("onorm", c)], tag=("onra", c))
        P.op("scalar", lambda e: e.activation(row(2), row(2), AF.Exp,
                                              scale=-1.0),
             waits=[("onra", c)], tag=("onr", c))
        P.op("vector", lambda e: e.tensor_scalar(
            row(2), row(2), MAXN, 1.0, ALU.mult, ALU.min),
            waits=[("onr", c)], tag=("f2", c))
        P.op("vector", lambda e: e.tensor_tensor(
            row(6), row(1), row(2), ALU.mult), waits=[("f2", c)],
            tag=("ff", c))
        w = [("ff", c)]
        w.append(("gfac", c - 1) if c > 0 else "M_rbz")
        P.op("gpsimd", lambda e: e.dma_start(
            out=s.rbscr_d[0:1, TL:2 * TL], in_=row(6)), waits=w,
            tag=("wfac", c), dma=True, stream="dr2")
        w = [("wfac", c), "M_idx1"]
        if c > 0:
            w += [("tmix", c - 1, NKT - 1), ("outtt", c - 1, NKT - 1)]
        P.op("gpsimd", lambda e: e.indirect_dma_start(
            out=s.rep_sb[:, 1, :], out_offset=None,
            in_=s.rb4,
            in_offset=bass.IndirectOffsetOnAxis(s.idx1[:, 0:1], axis=0)),
            waits=w, tag=("gfac", c), dma=True, stream="dh2")

    def emit_p5_kt(s, c, kt):
        P = s.P
        w = [("gfac", c), ("scl_ev", c, kt)]
        if kt >= 2:
            w.append(("r2", c, kt - 2))
        P.op("vector", lambda e, kt=kt: e.tensor_tensor(
            s.tf_2[:, kt % 2, :], s.e_b[:, kt, :], s.rep_sb[:, 1, :],
            ALU.mult), waits=w, tag=("tmix", c, kt))
        P.op("vector", lambda e, kt=kt: e.tensor_tensor(
            s.e_b[:, kt, :], s.tf_2[:, kt % 2, :], s.rc_b[:, kt, :],
            ALU.add),
            waits=[("tmix", c, kt), ("esq", c, kt), ("rc", c, kt)],
            tag=("r2", c, kt))
        w = [("r2", c, kt), ("sme", c, NKT - 1)]
        if kt >= 1:
            w.append(("smq2", c, kt - 1))
        P.op("scalar", lambda e, kt=kt: e.activation(
            s.esq_2[:], s.e_b[:, kt, :], AF.Square), waits=w,
            tag=("q2", c, kt))
        w = [("r2", c, kt)]
        if kt == 0:
            w.append(("s9ev", c))
        P.op("tensor", lambda e, kt=kt: e.matmul(
            s.ps_s6, s.onesc_b[:], s.e_b[:, kt, :],
            start=(kt == 0), stop=(kt == NKT - 1)), waits=w,
            tag=("smq", c, kt))
        w = [("q2", c, kt)]
        if kt == 0:
            w.append(("rows2", c))
        P.op("tensor", lambda e, kt=kt: e.matmul(
            s.ps_s7, s.onesc_b[:], s.esq_2[:],
            start=(kt == 0), stop=(kt == NKT - 1)), waits=w,
            tag=("smq2", c, kt))

    def emit_p5b(s, c):
        P = s.P
        row = s.row
        P.op("vector", lambda e: e.tensor_copy(row(0), s.ps_s6),
             waits=[("smq", c, NKT - 1)], tag=("s14ev", c))
        P.op("vector", lambda e: e.tensor_copy(row(1), s.ps_s7),
             waits=[("smq2", c, NKT - 1)], tag=("s15ev", c))
        P.op("vector", lambda e: e.tensor_scalar(
            row(2), row(0), RD, None, ALU.mult), waits=[("s14ev", c)],
            tag=("mean2", c))
        P.op("vector", lambda e: e.tensor_scalar(
            row(3), row(1), RD, None, ALU.mult), waits=[("s15ev", c)],
            tag=("eq2", c))
        P.op("vector", lambda e: e.tensor_tensor(
            row(4), row(2), row(2), ALU.mult), waits=[("mean2", c)],
            tag=("m22", c))
        P.op("vector", lambda e: e.tensor_tensor(
            row(4), row(3), row(4), ALU.subtract),
            waits=[("m22", c), ("eq2", c)], tag=("var2", c))
        P.op("scalar", lambda e: e.activation(
            row(3), row(4), AF.Ln, bias=s.c_eps[0:1, 0:1]),
            waits=[("var2", c), "M_eps"], tag=("lnv2", c))
        P.op("scalar", lambda e: e.activation(
            row(5), row(3), AF.Exp, scale=-0.5),
            waits=[("lnv2", c)], tag=("rstd2", c))
        P.op("vector", lambda e: e.tensor_tensor(
            row(6), row(2), row(5), ALU.mult),
            waits=[("rstd2", c), ("wfac", c)], tag=("mrs2", c))
        w = [("rstd2", c), ("mrs2", c)]
        w.append(("gln2", c - 1) if c > 0 else "M_rbz")
        P.op("gpsimd", lambda e: e.dma_start(
            out=s.rbscr_d[0:1, 2 * TL:4 * TL],
            in_=s.rows[0:1, 5:7, :].rearrange("o k t -> o (k t)")),
            waits=w, tag=("wln2", c), dma=True, stream="dr3")
        w = [("wln2", c), "M_idx1", ("rc", c, NDT - 1),
             ("tmix", c, NKT - 1)]
        if c > 0:
            w += [("o1", c - 1, NKT - 1), ("outtt", c - 1, NKT - 1)]
        P.op("gpsimd", lambda e: e.indirect_dma_start(
            out=s.rep_sb[:, 0:2, :].rearrange("p k t -> p (k t)"),
            out_offset=None, in_=s.rb2,
            in_offset=bass.IndirectOffsetOnAxis(s.idx1[:, 0:1], axis=0)),
            waits=w, tag=("gln2", c), dma=True, stream="dh3")

    def emit_out_kt(s, c, kt):
        P = s.P
        w = [("gln2", c), ("r2", c, kt), ("smq", c, kt), ("q2", c, kt)]
        if kt >= 2:
            w.append(("outtt", c, kt - 2))
        P.op("vector", lambda e, kt=kt: e.tensor_tensor(
            s.tf_2[:, kt % 2, :], s.e_b[:, kt, :], s.rep_sb[:, 0, :],
            ALU.mult), waits=w, tag=("o1", c, kt))
        w = [("o1", c, kt)]
        if kt >= 2:
            w.append(("st_out", c, kt - 2))
        elif c > 0:
            w.append(("st_out", c - 1, NKT - 2 + kt))
        P.op("vector", lambda e, kt=kt: e.tensor_tensor(
            s.out_2[:, kt % 2, :], s.tf_2[:, kt % 2, :], s.rep_sb[:, 1, :],
            ALU.subtract), waits=w, tag=("outtt", c, kt))
        P.op("scalar", lambda e, kt=kt: e.dma_start(
            out=s.out_d[kt * 128:(kt + 1) * 128, c * TL:(c + 1) * TL],
            in_=s.out_2[:, kt % 2, :]),
            waits=[("outtt", c, kt)], tag=("st_out", c, kt), dma=True,
            stream="dsA" if kt % 2 == 0 else "dsB")


_NC = None


def kernel(L0_attn_x, L0_attn_x_proj_w, L0_attn_dt_proj_w, L0_attn_dt_proj_b,
           L0_attn_A_log, L0_attn_D_param, L0_ffn_weight, L0_ffn_scale):
    global _NC
    x = np.asarray(L0_attn_x, np.float32)
    xpw = np.asarray(L0_attn_x_proj_w, np.float32)
    dtw = np.asarray(L0_attn_dt_proj_w, np.float32)
    dtb = np.asarray(L0_attn_dt_proj_b, np.float32)
    alog = np.asarray(L0_attn_A_log, np.float32)
    dp = np.asarray(L0_attn_D_param, np.float32)
    wf = np.asarray(L0_ffn_weight, np.float32)
    scl = np.asarray(L0_ffn_scale, np.float32)

    xpw_r = np.ascontiguousarray(
        xpw.T.reshape(NDT, 128, 160).transpose(1, 0, 2))
    dtw_r = np.ascontiguousarray(dtw.T)
    dtb_r = np.ascontiguousarray(dtb.reshape(NDT, 128).T)
    alog_r = np.ascontiguousarray(
        alog.reshape(NDT, 128, NST).transpose(1, 0, 2).reshape(128, NDT * NST))
    dp_r = np.ascontiguousarray(dp.reshape(NDT, 128).T)
    wsw = np.ascontiguousarray(
        wf.reshape(NDT, 128, NKT, 128).transpose(2, 1, 0, 3))
    scl_r = np.ascontiguousarray(scl.reshape(NKT, 128).T)
    eye = np.eye(128, dtype=ml_dtypes.bfloat16)
    onesc = np.ones((128, 1), dtype=ml_dtypes.bfloat16)

    if _NC is None:
        _NC = K().build()
    in_maps = []
    for cc in range(8):
        in_maps.append({
            "xT": np.ascontiguousarray(x[cc].T), "xpw": xpw_r, "dtw": dtw_r,
            "dtb": dtb_r, "alog": alog_r, "dp": dp_r, "wsw": wsw,
            "scl": scl_r, "eye": eye, "onesc": onesc,
        })
    res = run_bass_kernel_spmd(_NC, in_maps, core_ids=list(range(8)))
    out = np.stack([np.asarray(r["out"], np.float32).T
                    for r in res.results], axis=0)
    return np.ascontiguousarray(out)
